# version 12 (final)
# baseline (speedup 1.0000x reference)
import numpy as np
import ml_dtypes
from contextlib import ExitStack
import concourse.bass as bass
import concourse.mybir as mybir
from concourse.bass_utils import run_bass_kernel_spmd

F32 = mybir.dt.float32
BF16 = mybir.dt.bfloat16
I32 = mybir.dt.int32
AF = mybir.ActivationFunctionType
OP = mybir.AluOpType

NCORES = 8
T = 2048
D = 1024
H = 16
NP = 8
NT = 16
CH = 256
NCH = 8
I_FFN = 2816
NIC = 22
PR = 66
SCOLS = 8 * PR

BF = ml_dtypes.bfloat16


def _host_consts():
    c = {}
    c["IDENT"] = np.eye(128, dtype=BF)
    j = np.arange(128)[:, None]
    i = np.arange(128)[None, :]
    c["TRI"] = (i >= j).astype(BF)
    c["C_TRIH"] = ((j <= i).astype(np.float32) - 0.5).astype(BF)
    c["C_P05"] = np.full((128, 128), 0.5, BF)
    c["C_M05"] = np.full((128, 128), -0.5, BF)
    c["C_TRIS"] = (j > i).astype(BF)
    expc = np.zeros((16, 8, 128), np.float32)
    for cc in range(8):
        expc[2 * cc, cc, 0:64] = 1.0
        expc[2 * cc + 1, cc, 64:128] = 1.0
    c["EXPC"] = expc.reshape(16, 1024).astype(BF)
    ehc = np.zeros((128, 8 * 16), np.float32)
    for cc in range(8):
        ehc[:, 16 * cc:16 * cc + 16] = expc[:, cc, :].T
    c["EHC"] = ehc.astype(BF)
    expcol = np.zeros((16, 128), np.float32)
    for h in range(16):
        expcol[h, 64 * (h % 2):64 * (h % 2) + 64] = 1.0
    c["EXPCOL"] = expcol
    pm = np.zeros((16, 8), np.float32)
    for h in range(16):
        pm[h, h // 2] = 1.0
    c["PAIRMASK"] = pm
    c["E2"] = np.stack([(np.arange(128) < 64), (np.arange(128) >= 64)]).astype(BF)
    c["ONES2"] = np.stack([np.ones(128), np.zeros(128)], axis=1).astype(BF)
    c["ONESR2"] = np.ones((2, 128), np.float32).astype(BF)
    c["ONEF"] = np.ones((1, 128), np.float32)
    c["CONSTPOOL"] = np.tile(np.array([[1.0, -2.0, -6.0, 1e-5]], np.float32), (128, 1))
    return c


class Tk:
    def __init__(self, nc, ctx):
        self.nc = nc
        self.sems = {}
        self.count = {}
        self.waited = {}
        for s in ("pe", "ve", "se", "ge", "dsp", "dact", "dpool", "cc"):
            self.sems[s] = ctx.enter_context(nc.semaphore(s))
            self.count[s] = 0

    def inc(self, inst, stream, n=None):
        if n is None:
            n = 16 if stream.startswith("d") else 1
        inst.then_inc(self.sems[stream], n)
        self.count[stream] += n
        return self.count[stream]

    def tok(self, stream):
        return self.count[stream]

    def wait(self, eng, ename, stream, tokval=None):
        if tokval is None:
            tokval = self.count[stream]
        assert tokval <= self.count[stream]
        key = (ename, stream)
        if self.waited.get(key, 0) >= tokval or tokval <= 0:
            return
        eng.wait_ge(self.sems[stream], tokval)
        self.waited[key] = tokval


def build_program(debug_taps=(), upto=9, skip_ag=False, skip_wq=False, p3sub=9, p5sub=9,
                  barriers=("b34", "b45")):
    nc = bass.Bass("TRN2", target_bir_lowering=False)
    dt = nc.dram_tensor
    HC = _host_consts()

    x_in = dt("x_slice", [T, D], F32, kind="ExternalInput").ap()
    y_out = dt("y_slice", [T, D], F32, kind="ExternalOutput").ap()
    wq0 = dt("L0_attn_Wq", [D, D], F32, kind="ExternalInput").ap()
    wk0 = dt("L0_attn_Wk", [D, D], F32, kind="ExternalInput").ap()
    wv0 = dt("L0_attn_Wv", [D, D], F32, kind="ExternalInput").ap()
    wo0 = dt("L0_attn_Wo", [D, D], F32, kind="ExternalInput").ap()
    wa0 = dt("L0_attn_Wa", [D, H], F32, kind="ExternalInput").ap()
    ba0 = dt("ba_row2", [2, H], F32, kind="ExternalInput").ap()
    wg_sh = dt("wg_shard", [I_FFN, 128], F32, kind="ExternalInput").ap()
    wu_sh = dt("wu_shard", [I_FFN, 128], F32, kind="ExternalInput").ap()
    wd_sh = dt("wd_shard", [D, 352], F32, kind="ExternalInput").ap()
    wq1 = dt("L1_attn_Wq", [D, D], F32, kind="ExternalInput").ap()
    wk1 = dt("L1_attn_Wk", [D, D], F32, kind="ExternalInput").ap()
    wv1 = dt("L1_attn_Wv", [D, D], F32, kind="ExternalInput").ap()
    wo1 = dt("L1_attn_Wo", [D, D], F32, kind="ExternalInput").ap()
    bqc = dt("bq_col", [128, NP], F32, kind="ExternalInput").ap()
    bkc = dt("bk_col", [128, NP], F32, kind="ExternalInput").ap()
    bvc = dt("bv_col", [128, NP], F32, kind="ExternalInput").ap()
    wk1f = dt("L1_ffn_kernel", [D, D], F32, kind="ExternalInput").ap()
    bias2 = dt("bias2", [2, D], F32, kind="ExternalInput").ap()
    la1 = dt("L1_ffn_a", [D, H], F32, kind="ExternalInput").ap()
    lb1 = dt("L1_ffn_b", [H, D], F32, kind="ExternalInput").ap()
    cstd = {k: dt("c_" + k, list(v.shape),
                  BF16 if v.dtype == BF else F32, kind="ExternalInput").ap()
            for k, v in HC.items()}
    onehot = dt("partner_onehot", [128, 8], F32, kind="ExternalInput").ap()
    smask = dt("state_mask", [128, 1], F32, kind="ExternalInput").ap()

    ag_wg_in = dt("ag_wg_in", [128, I_FFN], BF16).ap()
    ag_wg_out = dt("ag_wg_out", [D, I_FFN], BF16).ap()
    ag_wu_in = dt("ag_wu_in", [128, I_FFN], BF16).ap()
    ag_wu_out = dt("ag_wu_out", [D, I_FFN], BF16).ap()
    ag_wd_in = dt("ag_wd_in", [352, D], BF16).ap()
    ag_wd_out = dt("ag_wd_out", [I_FFN, D], BF16).ap()
    rs_in = dt("rs_in", [NCORES * 128, SCOLS], F32).ap()
    rs_out = dt("rs_out", [128, SCOLS], F32).ap()
    gmax_b = dt("gmax_bounce", [1, 128], F32).ap()

    ctx = ExitStack()
    with ctx:
        def sb(name, p, f, d=BF16):
            return ctx.enter_context(nc.sbuf_tensor(name, [p, f], d))

        def psum(name, p, f):
            return ctx.enter_context(nc.psum_tensor(name, [p, f], F32))

        XBF = sb("XBF", 128, NT * D)
        POOL2 = sb("POOL2", 128, NT * D)
        POOL3 = sb("POOL3", 128, NT * D)
        POOL4 = sb("POOL4", 128, NT * D)
        WPOOL = sb("WPOOL", 128, 16 * D)
        WA = sb("WA", 128, 128)
        SNAPS = sb("SNAPS", 128, 7 * SCOLS)
        SWORK = sb("SWORK", 128, SCOLS, F32)
        SCORR = sb("SCORR", 128, SCOLS, F32)
        SGT = sb("SGT", 128, SCOLS)
        RSB = sb("RSB", 128, SCOLS, F32)
        ATIL = sb("ATIL", 16, NCH * CH)
        AEK = sb("AEK", 16, 512)
        M16 = sb("M16", 16, 32, F32)
        COLS_A = sb("COLS_A", 128, 128, F32)
        STATS = sb("STATS", 128, 64, F32)
        ROT1 = sb("ROT1", 128, D, F32)
        QSB1 = sb("QSB1", 128, D)
        QSB2 = sb("QSB2", 128, D)
        QSB3 = sb("QSB3", 128, D)
        QSM = sb("QSM", 128, D // 16, F32)
        QSMB = sb("QSMB", 128, D // 16)
        KHAT = sb("KHAT", 128, NP * CH)
        KHT = sb("KHT", 128, NP * CH)
        EBC = sb("EBC", 128, 256)
        PM = sb("PM", 128, 768)
        LAT = sb("LAT", 128, 2 * H, F32)
        LAH = sb("LAH", 128, 2 * H)
        LAL = sb("LAL", 128, 2 * H)
        RDE = sb("RDE", 128, 64, F32)
        RDT = sb("RDT", 2, 128)
        RDB = sb("RDB", 128, 96)
        BA2 = sb("BA2", 2, H)
        BQC = sb("BQC", 128, NP, F32)
        BKC = sb("BKC", 128, NP, F32)
        BVC = sb("BVC", 128, NP, F32)
        LRA = sb("LRA", 128, NP * H)
        OH = sb("OH", 128, 8, F32)
        SMK = sb("SMK", 128, 1, F32)
        CONS = {k: sb("k_" + k, v.shape[0], v.shape[1], BF16 if v.dtype == BF else F32)
                for k, v in HC.items()}

        SST = RSB
        ATT = KHT
        COEF = ATIL
        LRX = SNAPS[0:16, 0:T]
        LRB = SNAPS[0:16, T:T + D]
        B2T = KHAT[0:2, 0:D]
        GMR = RSB[0:1, 0:128]
        KSC = RSB[:, 128:130]
        GMX = RDE[:, 0:2]

        cp = CONS["CONSTPOOL"]
        nc.const_aps.register(F32, 1.0, cp[:, 0:1])
        nc.const_aps.register(F32, -2.0, cp[:, 1:2])
        nc.const_aps.register(F32, -6.0, cp[:, 2:3])
        nc.const_aps.register(F32, 1e-5, cp[:, 3:4])

        pA = psum("pA", 128, 512)
        pB = psum("pB", 128, 512)
        pC = psum("pC", 128, 1024)
        pD = psum("pD", 128, 512)
        pS = psum("pSU", 128, 1024)
        pX = psum("pX", 128, 512)

        VG = sb("VG", 128, 48, F32)
        block = ctx.enter_context(nc.Block())
        tk = Tk(nc, ctx)
        tapd = {}

        def vgap(v):
            inst = v.tensor_copy(out=VG[:, 0:16], in_=STATS[:, 0:16])
            tk.inc(inst, "ve")
            tk.wait(v, "ve", "ve", tk.tok("ve"))

        def tap(name, ap_fn, p, f, d=BF16):
            if name not in debug_taps:
                return
            tout = dt("tap_" + name, [p, f], d, kind="ExternalOutput").ap()

            @emit("gpsimd")
            def _(gp):
                tk.wait(gp, "ge", "ve", tk.tok("ve"))
                tk.wait(gp, "ge", "se", tk.tok("se"))
                tk.wait(gp, "ge", "pe", tk.tok("pe"))
                tk.inc(gp.dma_start(out=tout, in_=ap_fn()), "dpool")

        def emit(en):
            def deco(fn):
                getattr(block, en)(fn)
                return None
            return deco

        def barrier(name):
            if name not in barriers:
                return

            @emit("gpsimd")
            def _(gp):
                tk.wait(gp, "ge", "ve", tk.tok("ve"))
                tk.wait(gp, "ge", "se", tk.tok("se"))
                tk.wait(gp, "ge", "pe", tk.tok("pe"))
                tk.wait(gp, "ge", "ge", tk.tok("ge"))
                inst = gp.tensor_copy(out=BA2[0:1, 0:1], in_=BA2[0:1, 0:1])
                tk.inc(inst, "ge")

        @emit("sync")
        def _(sync):
            for k in CONS:
                tk.inc(sync.dma_start(out=CONS[k][:], in_=cstd[k]), "dsp")
            tk.inc(sync.dma_start(out=OH[:], in_=onehot), "dsp")
            tk.inc(sync.dma_start(out=SMK[:], in_=smask), "dsp")
            tk.inc(sync.dma_start(out=BQC[:], in_=bqc), "dsp")
            tk.inc(sync.dma_start(out=BKC[:], in_=bkc), "dsp")
            tk.inc(sync.dma_start(out=BVC[:], in_=bvc), "dsp")
            tk.inc(sync.dma_start(out=ROT1[0:2, 0:H], in_=ba0), "dsp")

        @emit("gpsimd")
        def _(g):
            tk.wait(g, "ge", "dsp")
            inst = g.tensor_copy(out=BA2[:, :], in_=ROT1[0:2, 0:H])
            tk.inc(inst, "ge")

        @emit("vector")
        def _(v):
            v.memset(SWORK[:], 0.0)
            v.memset(STATS[:], 0.0)
            v.memset(VG[:], 0.0)
            inst = v.memset(SCORR[:], 0.0)
            tk.inc(inst, "ve")

        wslot = {}

        def stage_weight(name, src_ap, slot, nrows=NP, width=D, dstbuf=None, after_pe=None):
            wslot[name] = (slot, width, dstbuf if dstbuf is not None else WPOOL)
            dst = wslot[name][2]
            cast_toks = []
            for r in range(nrows):
                rot = ROT1
                gate = tk.tok("ge") if r < 1 else cast_toks[r - 1]

                @emit("sync")
                def _(sync, r=r, rot=rot, gate=gate):
                    tk.wait(sync, "sync", "ge", gate)
                    tk.wait(sync, "sync", "se", tk.tok("se"))
                    tk.wait(sync, "sync", "ve", tk.tok("ve"))
                    inst = sync.dma_start(out=rot[:, 0:width],
                                          in_=src_ap[128 * r:128 * (r + 1), 0:width])
                    tk.inc(inst, "dsp")

                @emit("gpsimd")
                def _(g, r=r, rot=rot, slot=slot, width=width, dst=dst):
                    tk.wait(g, "ge", "dsp")
                    if after_pe is not None:
                        tk.wait(g, "ge", "pe", after_pe)
                    inst = g.tensor_copy(
                        out=dst[:, slot * D + r * width: slot * D + r * width + width],
                        in_=rot[:, 0:width])
                    cast_toks.append(tk.inc(inst, "ge"))

        def wtile(name, r):
            slot, width, buf = wslot[name]
            return buf[:, slot * D + r * width: slot * D + r * width + width]

        stage_weight("Wk0", wk0, 0)
        stage_weight("Wv0", wv0, 8)
        stage_weight("Wa0", wa0, 0, nrows=NP, width=H, dstbuf=WA)
        W_STAGED_0 = tk.tok("ge")

        def quant_pair(xt, F):
            st = {}
            nb = F // 16

            @emit("scalar")
            def _(s):
                tk.wait(s, "se", "se", tk.tok("se"))
                tk.wait(s, "se", "ve", tk.tok("ve"))
                inst = s.activation(out=QSB2[:, 0:F], in_=xt[:, 0:F], func=AF.Sign)
                st["s0"] = tk.inc(inst, "se")

            @emit("vector")
            def _(v):
                tk.wait(v, "ve", "ve", tk.tok("ve"))
                v.tensor_reduce(out=QSM[:, 0:nb],
                                in_=xt[:, 0:F].rearrange("p (n b) -> p n b", b=16),
                                axis=mybir.AxisListType.X, op=OP.max,
                                apply_absolute_value=True)
                vgap(v)
                v.tensor_scalar(out=QSM[:, 0:nb], in0=QSM[:, 0:nb],
                                scalar1=1.0 / 6.0, scalar2=1e-12,
                                op0=OP.mult, op1=OP.max)
                vgap(v)
                v.tensor_copy(out=QSMB[:, 0:nb], in_=QSM[:, 0:nb])
                vgap(v)
                inst = v.reciprocal(QSM[:, 0:nb], QSM[:, 0:nb])
                st["v1"] = tk.inc(inst, "ve")

            @emit("scalar")
            def _(s):
                tk.wait(s, "se", "ve", st["v1"])
                inst = s.activation(out=xt[:, 0:F], in_=xt[:, 0:F], func=AF.Abs)
                st["s1"] = tk.inc(inst, "se")

            @emit("vector")
            def _(v):
                tk.wait(v, "ve", "se", st["s1"])
                inst = v.tensor_tensor(
                    out=xt[:, 0:F].rearrange("p (n b) -> p n b", b=16),
                    in0=xt[:, 0:F].rearrange("p (n b) -> p n b", b=16),
                    in1=QSM[:, 0:nb][:, :, None].broadcast_to([128, nb, 16]),
                    op=OP.mult)
                st["v2"] = tk.inc(inst, "ve")

            @emit("vector")
            def _(v):
                def step(inst):
                    tk.inc(inst, "ve")
                    tk.wait(v, "ve", "ve", tk.tok("ve"))
                tk.wait(v, "ve", "ve", st["v2"])
                step(v.scalar_tensor_tensor(out=xt[:, 0:F], in0=xt[:, 0:F], scalar=2.0,
                                            in1=xt[:, 0:F], op0=OP.min, op1=OP.add))
                step(v.scalar_tensor_tensor(out=xt[:, 0:F], in0=xt[:, 0:F], scalar=6.0,
                                            in1=xt[:, 0:F], op0=OP.min, op1=OP.add))
                step(v.tensor_scalar_mul(xt[:, 0:F], xt[:, 0:F], 0.5))
                step(v.tensor_copy(out=xt[:, 0:F].bitcast(I32), in_=xt[:, 0:F]))
                inst = v.tensor_copy(out=QSB3[:, 0:F], in_=xt[:, 0:F].bitcast(I32))
                st["v3"] = tk.inc(inst, "ve")

            @emit("scalar")
            def _(s):
                tk.wait(s, "se", "ve", st["v3"])
                inst = s.activation(out=QSB1[:, 0:F], in_=QSB3[:, 0:F], func=AF.Relu,
                                    bias=-2.0, scale=0.5)
                st["s3"] = tk.inc(inst, "se")

            @emit("vector")
            def _(v):
                tk.wait(v, "ve", "se", st["s3"])
                inst = v.scalar_tensor_tensor(out=QSB1[:, 0:F], in0=QSB3[:, 0:F],
                                              scalar=0.5, in1=QSB1[:, 0:F],
                                              op0=OP.mult, op1=OP.add)
                st["v4"] = tk.inc(inst, "ve")

            @emit("scalar")
            def _(s):
                tk.wait(s, "se", "ve", st["v4"])
                inst = s.activation(out=QSB3[:, 0:F], in_=QSB3[:, 0:F], func=AF.Relu,
                                    bias=-6.0)
                st["s4"] = tk.inc(inst, "se")

            @emit("vector")
            def _(v):
                def step(inst):
                    tk.inc(inst, "ve")
                    tk.wait(v, "ve", "ve", tk.tok("ve"))
                tk.wait(v, "ve", "se", st["s4"])
                step(v.tensor_tensor(out=QSB3[:, 0:F], in0=QSB3[:, 0:F],
                                     in1=QSB1[:, 0:F], op=OP.add))
                step(v.tensor_tensor(out=QSB3[:, 0:F], in0=QSB3[:, 0:F],
                                     in1=QSB2[:, 0:F], op=OP.mult))
                inst = v.tensor_tensor(
                    out=QSB3[:, 0:F].rearrange("p (n b) -> p n b", b=16),
                    in0=QSB3[:, 0:F].rearrange("p (n b) -> p n b", b=16),
                    in1=QSMB[:, 0:nb][:, :, None].broadcast_to([128, nb, 16]),
                    op=OP.mult)
                st["out"] = tk.inc(inst, "ve")
            return st

        def shard_quant(src_ap, agin_ap, ntile, Fq):
            nrows_src = src_ap.shape[0]
            for t in range(ntile):
                rows = min(128, nrows_src - t * 128)

                @emit("sync")
                def _(sync, t=t, rows=rows):
                    tk.wait(sync, "sync", "ve", tk.tok("ve"))
                    tk.wait(sync, "sync", "ge", tk.tok("ge"))
                    tk.wait(sync, "sync", "se", tk.tok("se"))
                    inst = sync.dma_start(out=ROT1[0:rows, 0:Fq],
                                          in_=src_ap[t * 128:t * 128 + rows, 0:Fq])
                    tk.inc(inst, "dsp")
                dtok = tk.tok("dsp")

                @emit("vector")
                def _(v, dtok=dtok):
                    tk.wait(v, "ve", "dsp", dtok)
                    tk.wait(v, "ve", "dpool", tk.tok("dpool"))

                @emit("scalar")
                def _(s, dtok=dtok):
                    tk.wait(s, "se", "dsp", dtok)
                    tk.wait(s, "se", "dpool", tk.tok("dpool"))

                st = quant_pair(ROT1, Fq)
                nsub = (Fq + 127) // 128
                for s2 in range(nsub):
                    w2 = min(128, Fq - s2 * 128)

                    @emit("tensor")
                    def _(te, s2=s2, w2=w2, st=st):
                        tk.wait(te, "pe", "ve", tk.tok("ve"))
                        inst = te.matmul(pX[0:w2, 0:128],
                                         QSB3[:, s2 * 128:s2 * 128 + w2],
                                         CONS["IDENT"][:, :], start=True, stop=True)
                        tk.inc(inst, "pe")
                    ptok = tk.tok("pe")

                    @emit("vector")
                    def _(v, w2=w2, ptok=ptok):
                        tk.wait(v, "ve", "pe", ptok)
                        tk.wait(v, "ve", "dpool", tk.tok("dpool"))
                        inst = v.tensor_copy(out=QSB2[0:w2, 0:128], in_=pX[0:w2, 0:128])
                        tk.inc(inst, "ve")
                    etok = tk.tok("ve")

                    @emit("gpsimd")
                    def _(g, s2=s2, w2=w2, etok=etok, t=t):
                        tk.wait(g, "ge", "ve", etok)
                        inst = g.dma_start(
                            out=agin_ap[s2 * 128:s2 * 128 + w2, t * 128:t * 128 + 128],
                            in_=QSB2[0:w2, 0:128])
                        tk.inc(inst, "dpool")

        if not skip_wq:
            shard_quant(wg_sh, ag_wg_in, NIC, 128)
            shard_quant(wu_sh, ag_wu_in, NIC, 128)
            shard_quant(wd_sh, ag_wd_in, NP, 352)

        if not skip_ag:
            _dp = tk.tok("dpool")

            @emit("gpsimd")
            def _(g, _dp=_dp):
                tk.wait(g, "ge", "dpool", _dp)
                tk.inc(g.collective_compute("AllGather", OP.bypass,
                                            replica_groups=[list(range(NCORES))],
                                            ins=[ag_wg_in], outs=[ag_wg_out]), "cc")
                tk.inc(g.collective_compute("AllGather", OP.bypass,
                                            replica_groups=[list(range(NCORES))],
                                            ins=[ag_wu_in], outs=[ag_wu_out]), "cc")
                tk.inc(g.collective_compute("AllGather", OP.bypass,
                                            replica_groups=[list(range(NCORES))],
                                            ins=[ag_wd_in], outs=[ag_wd_out]), "cc")
        AG_DONE = tk.tok("cc")

        def ln_pass(stats_off, src_is_hbm):
            for t in range(NT):
                rot = ROT1
                if src_is_hbm:
                    @emit("sync")
                    def _(sync, t=t, rot=rot):
                        tk.wait(sync, "sync", "se", tk.tok("se"))
                        tk.wait(sync, "sync", "ve", tk.tok("ve"))
                        tk.wait(sync, "sync", "ge", tk.tok("ge"))
                        inst = sync.dma_start(out=rot[:, :], in_=x_in[128 * t:128 * (t + 1), :])
                        tk.inc(inst, "dsp")
                    dtok = tk.tok("dsp")
                    srcap = rot[:, :]
                else:
                    dtok = None
                    srcap = XBF[:, t * D:(t + 1) * D]

                @emit("vector")
                def _(v, t=t, srcap=srcap, dtok=dtok):
                    if dtok:
                        tk.wait(v, "ve", "dsp", dtok)
                    inst = v.reduce_sum(out=STATS[:, stats_off + t:stats_off + t + 1],
                                        in_=srcap, axis=mybir.AxisListType.X)
                    tk.inc(inst, "ve")

                @emit("scalar")
                def _(s, t=t, srcap=srcap, dtok=dtok):
                    if dtok:
                        tk.wait(s, "se", "dsp", dtok)
                    else:
                        tk.wait(s, "se", "ve", tk.tok("ve"))
                    inst = s.activation(out=QSB1[:, :], in_=srcap, func=AF.Square)
                    tk.inc(inst, "se")
                sqtok = tk.tok("se")

                @emit("vector")
                def _(v, t=t, sqtok=sqtok):
                    tk.wait(v, "ve", "se", sqtok)
                    inst = v.reduce_sum(out=STATS[:, stats_off + NT + t:stats_off + NT + t + 1],
                                        in_=QSB1[:, :], axis=mybir.AxisListType.X)
                    tk.inc(inst, "ve")

                if src_is_hbm:
                    @emit("vector")
                    def _(v, t=t, rot=rot):
                        inst = v.tensor_copy(out=XBF[:, t * D:(t + 1) * D], in_=rot[:, :])
                        tk.inc(inst, "ve")

            @emit("vector")
            def _(v):
                tk.wait(v, "ve", "se", tk.tok("se"))
                tk.wait(v, "ve", "ve", tk.tok("ve"))
                v.tensor_scalar_mul(QSM[:, 0:NT], STATS[:, stats_off:stats_off + NT], 1.0 / D)
                vgap(v)
                v.tensor_mul(QSM[:, NT:2 * NT], QSM[:, 0:NT], QSM[:, 0:NT])
                vgap(v)
                inst = v.scalar_tensor_tensor(out=STATS[:, stats_off + 2 * NT:stats_off + 3 * NT],
                                              in0=STATS[:, stats_off + NT:stats_off + 2 * NT],
                                              scalar=1.0 / D, in1=QSM[:, NT:2 * NT],
                                              op0=OP.mult, op1=OP.subtract)
                tk.inc(inst, "ve")

            if "lnchain" in debug_taps:
                @emit("vector")
                def _(v):
                    inst = v.tensor_copy(out=RDE[:, 32:64], in_=QSM[:, 0:32])
                    tk.inc(inst, "ve")

            if "lnchain" in debug_taps:
                @emit("vector")
                def _(v):
                    inst = v.tensor_copy(out=RDE[:, 0:16], in_=STATS[:, stats_off + 2 * NT:stats_off + 3 * NT])
                    tk.inc(inst, "ve")

            @emit("scalar")
            def _(s):
                tk.wait(s, "se", "ve", tk.tok("ve"))
                inst = s.activation(out=STATS[:, stats_off + 2 * NT:stats_off + 3 * NT],
                                    in_=STATS[:, stats_off + 2 * NT:stats_off + 3 * NT],
                                    func=AF.Ln, bias=1e-5)
                tk.inc(inst, "se")
                tk.wait(s, "se", "se", tk.tok("se"))
                inst = s.activation(out=STATS[:, stats_off + 2 * NT:stats_off + 3 * NT],
                                    in_=STATS[:, stats_off + 2 * NT:stats_off + 3 * NT],
                                    func=AF.Exp, scale=-0.5)
                tk.inc(inst, "se")

            if "lnchain" in debug_taps:
                @emit("vector")
                def _(v):
                    tk.wait(v, "ve", "se", tk.tok("se"))
                    inst = v.tensor_copy(out=RDE[:, 16:32], in_=STATS[:, stats_off + 2 * NT:stats_off + 3 * NT])
                    tk.inc(inst, "ve")

            @emit("vector")
            def _(v):
                tk.wait(v, "ve", "se", tk.tok("se"))
                v.tensor_mul(QSM[:, 0:NT], QSM[:, 0:NT],
                             STATS[:, stats_off + 2 * NT:stats_off + 3 * NT])
                vgap(v)
                inst = v.tensor_scalar_mul(STATS[:, stats_off + 3 * NT:stats_off + 4 * NT],
                                           QSM[:, 0:NT], -1.0)
                tk.inc(inst, "ve")

        def rstd_col(stats_off, t):
            return STATS[:, stats_off + 2 * NT + t:stats_off + 2 * NT + t + 1]

        def nmr_col(stats_off, t):
            return STATS[:, stats_off + 3 * NT + t:stats_off + 3 * NT + t + 1]

        def build_fm(dst_pool, stats_off, src_hbm, src_sb):
            for t in range(NT):
                rot = ROT1
                if src_hbm is not None:
                    @emit("sync")
                    def _(sync, t=t, rot=rot):
                        tk.wait(sync, "sync", "se", tk.tok("se"))
                        tk.wait(sync, "sync", "ge", tk.tok("ge"))
                        inst = sync.dma_start(out=rot[:, :], in_=src_hbm[128 * t:128 * (t + 1), :])
                        tk.inc(inst, "dsp")
                    dtok = tk.tok("dsp")
                else:
                    dtok = None

                @emit("scalar")
                def _(s, t=t, rot=rot, dtok=dtok):
                    if dtok:
                        tk.wait(s, "se", "dsp", dtok)
                    tk.wait(s, "se", "pe", tk.tok("pe"))
                    tk.wait(s, "se", "ve", tk.tok("ve"))
                    srcap = rot[:, :] if src_hbm is not None else src_sb[:, t * D:(t + 1) * D]
                    inst = s.activation(out=QSB1[:, :], in_=srcap, func=AF.Identity,
                                        bias=nmr_col(stats_off, t), scale=rstd_col(stats_off, t))
                    tk.inc(inst, "se")
                stok = tk.tok("se")

                for hh in range(2):
                    @emit("tensor")
                    def _(te, t=t, hh=hh, stok=stok):
                        tk.wait(te, "pe", "se", stok)
                        tk.wait(te, "pe", "ve", tk.tok("ve"))
                        inst = None
                        for q in range(4):
                            dc = hh * 4 + q
                            inst = te.matmul(pX[:, q * 128:(q + 1) * 128],
                                             QSB1[:, dc * 128:(dc + 1) * 128],
                                             CONS["IDENT"][:, :], start=True, stop=True)
                        tk.inc(inst, "pe")
                    ptok = tk.tok("pe")

                    @emit("vector")
                    def _(v, t=t, hh=hh, ptok=ptok):
                        tk.wait(v, "ve", "pe", ptok)
                        inst = None
                        for q in range(4):
                            dc = hh * 4 + q
                            inst = v.tensor_copy(
                                out=dst_pool[:, dc * T + t * 128:dc * T + (t + 1) * 128],
                                in_=pX[:, q * 128:(q + 1) * 128])
                        tk.inc(inst, "ve")

        ln_pass(0, True)
        tap("stats", lambda: STATS[:, 0:64], 128, 64, F32)
        tap("lnchain", lambda: RDE[:, 0:64], 128, 64, F32)
        build_fm(POOL2, 0, x_in, None)
        XN_FM_DONE_VE = tk.tok("ve")
        tap("xnfm", lambda: POOL2[:, :], 128, NT * D)
        tap("xbf1", lambda: XBF[:, :], 128, NT * D)

        env = dict(locals())
        if upto >= 2:
            _emit_phase2(nc, block, tk, emit, env)
        tap("ktil", lambda: POOL3[:, :], 128, NT * D)
        tap("vt", lambda: POOL4[:, :], 128, NT * D)
        tap("snaps", lambda: SNAPS[:, :], 128, 7 * SCOLS)
        tap("swork", lambda: SWORK[:, :], 128, SCOLS, F32)
        tap("scorr", lambda: SCORR[:, :], 128, SCOLS, F32)
        tap("atil", lambda: ATIL[:, :], 16, NCH * CH)
        tap("colsa", lambda: COLS_A[:, :], 128, 128, F32)
        if upto >= 3:
            env['p3sub'] = p3sub
            _emit_phase3(nc, block, tk, emit, env)
        tap("sgt", lambda: SGT[:, :], 128, SCOLS)
        tap("lat", lambda: LAT[:, :], 128, 32, F32)
        tap("lah", lambda: LAH[:, :], 128, 32)
        tap("lal", lambda: LAL[:, :], 128, 32)
        tap("aek", lambda: AEK[:, :], 16, 512)
        tap("xbf3", lambda: XBF[:, :], 128, NT * D)
        barrier("b34")
        if upto >= 4:
            _emit_phase4(nc, block, tk, emit, env)
        tap("xbf4", lambda: XBF[:, :], 128, NT * D)
        barrier("b45")
        tap("xq", lambda: POOL2[:, :], 128, NT * D)
        tap("hq", lambda: POOL3[:, :], 128, NT * D)
        if upto >= 5:
            env['p5sub'] = p5sub
            _emit_phase5(nc, block, tk, emit, env)
        tap("xn2end", lambda: POOL2[:, :], 128, NT * D)
        tap("xbfend", lambda: XBF[:, :], 128, NT * D)
        tap("fqpend", lambda: POOL4[:, :], 128, NT * D)
        tap("statsend", lambda: STATS[:, 0:64], 128, 64, F32)
        tap("lrxend", lambda: LRX[:, :], 16, T)
        if upto < 5:
            @emit("vector")
            def _(v):
                tk.wait(v, "ve", "pe", tk.tok("pe"))
                tk.wait(v, "ve", "se", tk.tok("se"))
                inst = v.tensor_copy(out=ROT1[:, :], in_=XBF[:, 0:D])
                tk.inc(inst, "ve")
            _e = tk.tok("ve")

            @emit("gpsimd")
            def _(gp, _e=_e):
                tk.wait(gp, "ge", "ve", _e)
                for t in range(NT):
                    tk.inc(gp.dma_start(out=y_out[128 * t:128 * (t + 1), :], in_=ROT1[:, :]), "dpool")

        @emit("gpsimd")
        def _(g):
            tk.wait(g, "ge", "dpool", tk.tok("dpool"))
            tk.wait(g, "ge", "dsp", tk.tok("dsp"))

    return nc, dict(tk_counts=dict(tk.count), taps=list(tapd))


def _emit_phase2(nc, block, tk, emit, g):
    @emit("vector")
    def _(v):
        tk.wait(v, "ve", "dpool", tk.tok("dpool"))

    @emit("scalar")
    def _(s):
        tk.wait(s, "se", "dpool", tk.tok("dpool"))

    XN, KTIL, VT = g["POOL2"], g["POOL3"], g["POOL4"]
    KHAT, KHT, EBC = g["KHAT"], g["KHT"], g["EBC"]
    ATIL, AEK, M16 = g["ATIL"], g["AEK"], g["M16"]
    COLS_A, LAT, LAH, LAL, RDE = g["COLS_A"], g["LAT"], g["LAH"], g["LAL"], g["RDE"]
    SWORK, SNAPS, SCORR, SST = g["SWORK"], g["SNAPS"], g["SCORR"], g["SST"]
    RSB, OH, SMK, BA2, WA = g["RSB"], g["OH"], g["SMK"], g["BA2"], g["WA"]
    CONS, wtile, vgap = g["CONS"], g["wtile"], g["vgap"]
    QSB1, QSB2, QSB3 = g["QSB1"], g["QSB2"], g["QSB3"]
    pA, pB, pC, pD, pS, pX = g["pA"], g["pB"], g["pC"], g["pD"], g["pS"], g["pX"]
    rs_in, rs_out = g["rs_in"], g["rs_out"]
    debug_taps, g_SGT = g["debug_taps"], g["SGT"]
    W_STAGED_0, XN_FM_DONE_VE = g["W_STAGED_0"], g["XN_FM_DONE_VE"]

    for c in range(NCH):
        for js in range(2):
            ti = 2 * c + js

            @emit("tensor")
            def _(te, ti=ti):
                tk.wait(te, "pe", "ve", g["XN_FM_DONE_VE"])
                tk.wait(te, "pe", "ge", W_STAGED_0)
                tk.wait(te, "pe", "ve", tk.tok("ve"))
                tk.wait(te, "pe", "se", tk.tok("se"))
                inst = None
                for dc in range(NP):
                    lhs = XN[:, dc * T + ti * 128: dc * T + (ti + 1) * 128]
                    te.matmul(pC[:, 0:512], lhs, wtile("Wv0", dc)[:, 0:512],
                              start=(dc == 0), stop=(dc == NP - 1))
                    te.matmul(pC[:, 512:1024], lhs, wtile("Wv0", dc)[:, 512:1024],
                              start=(dc == 0), stop=(dc == NP - 1))
                    te.matmul(pD[:, 0:16], lhs, WA[:, dc * 16:(dc + 1) * 16],
                              start=(dc == 0), stop=False)
                inst = te.matmul(pD[:, 0:16], CONS["ONESR2"][:, :], BA2[:, 0:16],
                                 start=False, stop=True)
                tk.inc(inst, "pe")
            ptok = tk.tok("pe")

            @emit("vector")
            def _(v, ti=ti, ptok=ptok):
                tk.wait(v, "ve", "pe", ptok)
                v.tensor_copy(out=VT[:, ti * D:(ti + 1) * D], in_=pC[:, :])
                inst = v.tensor_scalar_min(RDE[:, 0:16], pD[:, 0:16], 0.0)
                tk.inc(inst, "ve")
            vtok = tk.tok("ve")

            @emit("scalar")
            def _(s, ptok=ptok, vtok=vtok):
                tk.wait(s, "se", "pe", ptok)
                tk.wait(s, "se", "ve", vtok)
                inst = s.activation(out=RDE[:, 16:32], in_=pD[:, 0:16], func=AF.Abs)
                tk.inc(inst, "se")
                tk.wait(s, "se", "se", tk.tok("se"))
                inst = s.activation(out=RDE[:, 16:32], in_=RDE[:, 16:32], func=AF.Exp, scale=-1.0)
                tk.inc(inst, "se")
                tk.wait(s, "se", "se", tk.tok("se"))
                inst = s.activation(out=RDE[:, 16:32], in_=RDE[:, 16:32], func=AF.Ln, bias=1.0)
                tk.inc(inst, "se")
            stok = tk.tok("se")

            @emit("vector")
            def _(v, js=js, stok=stok):
                tk.wait(v, "ve", "se", stok)
                sl = slice(js * 16, (js + 1) * 16)
                v.tensor_sub(LAT[:, sl], RDE[:, 0:16], RDE[:, 16:32])
                vgap(v)
                inst = v.tensor_copy(out=LAH[:, sl], in_=LAT[:, sl])
                tk.inc(inst, "ve")
            _lah = tk.tok("ve")

            @emit("scalar")
            def _(s, js=js, _lah=_lah):
                tk.wait(s, "se", "ve", _lah)
                inst = s.copy(out=RDE[:, 32:48], in_=LAH[:, js * 16:(js + 1) * 16])
                tk.inc(inst, "se")
            _rde = tk.tok("se")

            @emit("vector")
            def _(v, js=js, _rde=_rde):
                tk.wait(v, "ve", "se", _rde)
                sl = slice(js * 16, (js + 1) * 16)
                inst = v.tensor_sub(RDE[:, 32:48], LAT[:, sl], RDE[:, 32:48])
                tk.inc(inst, "ve")
            _lo = tk.tok("ve")

            @emit("scalar")
            def _(s, js=js, _lo=_lo):
                tk.wait(s, "se", "ve", _lo)
                inst = s.copy(out=LAL[:, js * 16:(js + 1) * 16], in_=RDE[:, 32:48])
                tk.inc(inst, "se")

        @emit("tensor")
        def _(te, c=c):
            tk.wait(te, "pe", "ve", tk.tok("ve"))
            tk.wait(te, "pe", "se", tk.tok("se"))
            regions = {
                0: [("C_TRIH", 0, 0), ("C_M05", 0, 1), ("C_TRIH", 1, 0), ("C_M05", 1, 1)],
                1: [("C_P05", 0, 0), ("C_TRIH", 0, 1), ("C_P05", 1, 0), ("C_TRIH", 1, 1)],
                2: [("C_TRIS", 0, 0), ("C_P05", 0, 1), ("C_P05", 0, 1), ("C_TRIS", 1, 0),
                    ("C_P05", 1, 1), ("C_P05", 1, 1)],
                3: [("C_TRIS", 0, 1), ("C_TRIS", 1, 1)],
            }
            inst = None
            for r, ops in regions.items():
                n = len(ops)
                for k, (cn, lx, half) in enumerate(ops):
                    lax = (LAH, LAL)[lx]
                    la = lax[:, 0:16] if half == 0 else lax[:, 16:32]
                    inst = te.matmul(pD[0:16, r * 128:(r + 1) * 128], la, CONS[cn][:, :],
                                     start=(k == 0), stop=(k == n - 1))
            tk.inc(inst, "pe")
        ptok = tk.tok("pe")

        @emit("scalar")
        def _(s, c=c, ptok=ptok):
            tk.wait(s, "se", "pe", ptok)
            tk.wait(s, "se", "ve", tk.tok("ve"))
            if c == 7 and "sgt" in debug_taps:
                s.copy(out=g_SGT[0:16, 0:512], in_=pD[0:16, 0:512])
            s.activation(out=ATIL[:, c * CH:(c + 1) * CH], in_=pD[0:16, 0:256], func=AF.Exp)
            s.activation(out=M16[:, 17:18], in_=pD[0:16, 255:256], func=AF.Exp)
            s.activation(out=AEK[:, 0:256], in_=pD[0:16, 0:256], func=AF.Exp, scale=-1.0)
            inst = s.activation(out=AEK[:, 256:512], in_=pD[0:16, 256:512], func=AF.Exp)
            tk.inc(inst, "se")
        stok = tk.tok("se")

        @emit("scalar")
        def _(s, c=c, stok=stok):
            tk.wait(s, "se", "se", stok)
            inst = s.activation(out=M16[:, 16:17], in_=M16[:, 17:18], func=AF.Square)
            tk.inc(inst, "se")
        sq16 = tk.tok("se")

        @emit("vector")
        def _(v, c=c, sq16=sq16):
            tk.wait(v, "ve", "se", sq16)
            v.tensor_tensor(out=M16[:, 0:8], in0=CONS["PAIRMASK"][:, :],
                            in1=M16[:, 16:17].broadcast_to([16, 8]), op=OP.mult)
            inst = v.tensor_tensor(out=M16[:, 8:16], in0=CONS["PAIRMASK"][:, :],
                                   in1=M16[:, 17:18].broadcast_to([16, 8]), op=OP.mult)
            tk.inc(inst, "ve")
        vtok = tk.tok("ve")

        @emit("tensor")
        def _(te, c=c, vtok=vtok):
            tk.wait(te, "pe", "ve", vtok)
            te.matmul(pX[:, 0:8], CONS["EXPCOL"][:, :], M16[:, 0:8], start=True, stop=True)
            inst = te.matmul(pX[:, 8:16], CONS["EXPCOL"][:, :], M16[:, 8:16], start=True, stop=True)
            tk.inc(inst, "pe")
        ptok2 = tk.tok("pe")

        @emit("scalar")
        def _(s, c=c, ptok2=ptok2):
            tk.wait(s, "se", "pe", ptok2)
            s.copy(out=COLS_A[:, 16 * c:16 * c + 16], in_=pX[:, 0:16])
            s.copy(out=RDE[:, 32:48], in_=RDE[:, 0:16])
            inst = s.copy(out=RDE[:, 48:64], in_=RDE[:, 16:32])
            tk.inc(inst, "se")

        for of in range(NP):
            @emit("tensor")
            def _(te, c=c, of=of):
                tk.wait(te, "pe", "ve", tk.tok("ve"))
                inst = None
                for dc in range(NP):
                    inst = te.matmul(pA[:, 0:256],
                                     wtile("Wk0", dc)[:, of * 128:(of + 1) * 128],
                                     XN[:, dc * T + c * CH: dc * T + (c + 1) * CH],
                                     start=(dc == 0), stop=(dc == NP - 1))
                inst = te.matmul(pB[:, 0:512], CONS["EXPC"][:, of * 128:(of + 1) * 128],
                                 AEK[:, 0:512], start=True, stop=True)
                tk.inc(inst, "pe")
            kptok = tk.tok("pe")

            @emit("scalar")
            def _(s, of=of, kptok=kptok):
                tk.wait(s, "se", "pe", kptok)
                inst = s.activation(out=QSB1[:, 0:256], in_=pA[:, 0:256], func=AF.Exp)
                tk.inc(inst, "se")
            ketok = tk.tok("se")

            @emit("vector")
            def _(v, c=c, of=of, kptok=kptok, ketok=ketok):
                def fence(inst):
                    tk.inc(inst, "ve")
                    tk.wait(v, "ve", "ve", tk.tok("ve"))
                tk.wait(v, "ve", "pe", kptok)
                fence(v.tensor_scalar_max(QSB2[:, 0:256], pA[:, 0:256], 0.0))
                tk.wait(v, "ve", "se", ketok)
                fence(v.scalar_tensor_tensor(out=QSB3[:, 0:256], in0=QSB1[:, 0:256],
                                             scalar=1.0, in1=QSB2[:, 0:256],
                                             op0=OP.min, op1=OP.add))
                fence(v.tensor_copy(out=EBC[:, 0:256], in_=pB[:, 0:256]))
                fence(v.tensor_tensor(out=KTIL[:, of * T + c * CH: of * T + (c + 1) * CH],
                                      in0=QSB3[:, 0:256], in1=EBC[:, 0:256], op=OP.mult))
                fence(v.tensor_copy(out=EBC[:, 0:256], in_=pB[:, 256:512]))
                inst = v.tensor_tensor(out=KHAT[:, of * CH:(of + 1) * CH],
                                       in0=QSB3[:, 0:256], in1=EBC[:, 0:256], op=OP.mult)
                tk.inc(inst, "ve")

        for dc in range(NP):
            for js in range(2):
                @emit("tensor")
                def _(te, dc=dc, js=js):
                    tk.wait(te, "pe", "ve", tk.tok("ve"))
                    inst = te.matmul(pX[:, 128:256],
                                     KHAT[:, dc * CH + js * 128: dc * CH + (js + 1) * 128],
                                     CONS["IDENT"][:, :], start=True, stop=True)
                    tk.inc(inst, "pe")
                ptok4 = tk.tok("pe")

                @emit("vector")
                def _(v, dc=dc, js=js, ptok4=ptok4):
                    tk.wait(v, "ve", "pe", ptok4)
                    inst = v.tensor_copy(out=KHT[:, dc * CH + js * 128: dc * CH + (js + 1) * 128],
                                         in_=pX[:, 128:256])
                    tk.inc(inst, "ve")
        khtok = tk.tok("ve")

        @emit("tensor")
        def _(te, c=c, khtok=khtok):
            tk.wait(te, "pe", "ve", khtok)
            inst = None
            for p in range(NP):
                for h in range(2):
                    for js in range(2):
                        lhs = KHT[:, p * CH + js * 128 + 64 * h: p * CH + js * 128 + 64 * h + 64]
                        te.matmul(pS[64 * h:64 * h + 64, 128 * p:128 * p + 64],
                                  lhs,
                                  VT[:, (2 * c + js) * D + (2 * p + h) * 64:
                                     (2 * c + js) * D + (2 * p + h) * 64 + 64],
                                  start=(js == 0), stop=(js == 1),
                                  tile_position=(0, 64 * h))
                    for js in range(2):
                        lhs = KHT[:, p * CH + js * 128 + 64 * h: p * CH + js * 128 + 64 * h + 64]
                        inst = te.matmul(pS[64 * h:64 * h + 64, 128 * p + 64:128 * p + 66],
                                         lhs, CONS["ONES2"][:, :],
                                         start=(js == 0), stop=(js == 1),
                                         tile_position=(0, 64 * h))
            tk.inc(inst, "pe")
        ptok5 = tk.tok("pe")

        @emit("vector")
        def _(v, c=c, ptok5=ptok5):
            tk.wait(v, "ve", "pe", ptok5)
            tk.wait(v, "ve", "se", tk.tok("se"))
            inst = v.tensor_tensor(out=SST[:, :].rearrange("p (g s) -> p g s", s=PR),
                                   in0=SWORK[:, :].rearrange("p (g s) -> p g s", s=PR),
                                   in1=COLS_A[:, 16 * c:16 * c + 8][:, :, None].broadcast_to([128, 8, PR]),
                                   op=OP.mult)
            tk.inc(inst, "ve")
            tk.wait(v, "ve", "ve", tk.tok("ve"))
            inst = v.tensor_tensor(out=SWORK[:, :].rearrange("p (g s) -> p g s", s=PR),
                                   in0=SST[:, :].rearrange("p (g s) -> p g s", s=PR),
                                   in1=pS[:, :].rearrange("p (g s) -> p g s", s=128)[:, :, 0:PR],
                                   op=OP.add)
            tk.inc(inst, "ve")
            tk.wait(v, "ve", "ve", tk.tok("ve"))
            if c < 7:
                inst = v.tensor_copy(out=SNAPS[:, c * SCOLS:(c + 1) * SCOLS], in_=SWORK[:, :])
                tk.inc(inst, "ve")

    for b in range(NCORES):
        @emit("vector")
        def _(v, b=b):
            tk.wait(v, "ve", "dpool", tk.tok("dpool"))
            tk.wait(v, "ve", "ve", tk.tok("ve"))
            inst = v.tensor_scalar_mul(RSB[:, :], SWORK[:, :], OH[:, b:b + 1])
            tk.inc(inst, "ve")
        vtok = tk.tok("ve")

        @emit("gpsimd")
        def _(gp, b=b, vtok=vtok):
            tk.wait(gp, "ge", "ve", vtok)
            inst = gp.dma_start(out=rs_in[b * 128:(b + 1) * 128, :], in_=RSB[:, :])
            tk.inc(inst, "dpool")
    _dp = tk.tok("dpool")

    @emit("gpsimd")
    def _(gp, _dp=_dp):
        tk.wait(gp, "ge", "dpool", _dp)
        inst = gp.collective_compute("ReduceScatter", OP.add,
                                     replica_groups=[list(range(NCORES))],
                                     ins=[rs_in], outs=[rs_out])
        tk.inc(inst, "cc")
        tk.wait(gp, "ge", "cc", tk.tok("cc"))
        inst = gp.dma_start(out=RSB[:, :], in_=rs_out)
        tk.inc(inst, "dpool")
    _dp2 = tk.tok("dpool")

    @emit("vector")
    def _(v, _dp2=_dp2):
        tk.wait(v, "ve", "dpool", _dp2)
        inst = v.tensor_scalar_mul(SCORR[:, :], RSB[:, :], SMK[:, 0:1])
        tk.inc(inst, "ve")


def _emit_phase3(nc, block, tk, emit, g):
    XN, KTIL, VT = g["POOL2"], g["POOL3"], g["POOL4"]
    KHAT, EBC, PM = g["KHAT"], g["EBC"], g["PM"]
    ATT = g["KHT"]
    ATIL = g["ATIL"]
    COLS_A, RDE, RDT, RDB = g["COLS_A"], g["RDE"], g["RDT"], g["RDB"]
    ROT1 = g["ROT1"]
    SNAPS, SCORR, SST, SGT = g["SNAPS"], g["SCORR"], g["SST"], g["SGT"]
    XBF, CONS, wtile, vgap = g["XBF"], g["CONS"], g["wtile"], g["vgap"]
    QSB1, QSB2, QSB3 = g["QSB1"], g["QSB2"], g["QSB3"]
    pA, pB, pC, pD, pS, pX = g["pA"], g["pB"], g["pC"], g["pD"], g["pS"], g["pX"]

    p3sub = g.get("p3sub", 9)
    g["stage_weight"]("Wq0", g["wq0"], 0, after_pe=tk.tok("pe"))
    g["stage_weight"]("Wo0", g["wo0"], 8, after_pe=tk.tok("pe"))
    W_STAGED_1 = tk.tok("ge")

    for c in range(NCH):
        for of in range(NP):
            @emit("tensor")
            def _(te, c=c, of=of):
                tk.wait(te, "pe", "ge", W_STAGED_1)
                tk.wait(te, "pe", "ve", tk.tok("ve"))
                inst = None
                for dc in range(NP):
                    inst = te.matmul(pA[:, 0:256],
                                     wtile("Wq0", dc)[:, of * 128:(of + 1) * 128],
                                     XN[:, dc * T + c * CH: dc * T + (c + 1) * CH],
                                     start=(dc == 0), stop=(dc == NP - 1))
                inst = te.matmul(pB[:, 0:256], CONS["EXPC"][:, of * 128:(of + 1) * 128],
                                 ATIL[:, c * CH:(c + 1) * CH], start=True, stop=True)
                tk.inc(inst, "pe")
            qptok = tk.tok("pe")

            @emit("scalar")
            def _(s, of=of, qptok=qptok):
                tk.wait(s, "se", "pe", qptok)
                inst = s.activation(out=QSB1[:, 0:256], in_=pA[:, 0:256], func=AF.Exp)
                tk.inc(inst, "se")
            qetok = tk.tok("se")

            @emit("vector")
            def _(v, c=c, of=of, qptok=qptok, qetok=qetok):
                def fence(inst):
                    tk.inc(inst, "ve")
                    tk.wait(v, "ve", "ve", tk.tok("ve"))
                tk.wait(v, "ve", "pe", qptok)
                fence(v.tensor_scalar_max(QSB2[:, 0:256], pA[:, 0:256], 0.0))
                fence(v.tensor_copy(out=EBC[:, 0:256], in_=pB[:, 0:256]))
                tk.wait(v, "ve", "se", qetok)
                fence(v.scalar_tensor_tensor(out=QSB3[:, 0:256], in0=QSB1[:, 0:256],
                                             scalar=1.0, in1=QSB2[:, 0:256],
                                             op0=OP.min, op1=OP.add))
                inst = v.tensor_tensor(out=KHAT[:, of * CH:(of + 1) * CH],
                                       in0=QSB3[:, 0:256], in1=EBC[:, 0:256], op=OP.mult)
                tk.inc(inst, "ve")

        if p3sub < 2:
            continue
        @emit("vector")
        def _(v, c=c):
            tk.wait(v, "ve", "se", tk.tok("se"))
            if c == 0:
                inst = v.tensor_copy(out=SST[:, :], in_=SCORR[:, :])
            else:
                inst = v.tensor_tensor(out=SST[:, :], in0=SCORR[:, :],
                                       in1=SNAPS[:, (c - 1) * SCOLS:c * SCOLS], op=OP.add)
            tk.inc(inst, "ve")
            tk.wait(v, "ve", "ve", tk.tok("ve"))
            v.tensor_tensor(out=SGT[:, :].rearrange("p (g s) -> p g s", s=PR),
                            in0=SST[:, :].rearrange("p (g s) -> p g s", s=PR),
                            in1=COLS_A[:, 16 * c + 8:16 * c + 16][:, :, None].broadcast_to([128, 8, PR]),
                            op=OP.mult)
            inst = v.tensor_tensor(out=SCORR[:, :].rearrange("p (g s) -> p g s", s=PR),
                                   in0=SCORR[:, :].rearrange("p (g s) -> p g s", s=PR),
                                   in1=COLS_A[:, 16 * c:16 * c + 8][:, :, None].broadcast_to([128, 8, PR]),
                                   op=OP.mult)
            tk.inc(inst, "ve")

        if p3sub < 3:
            continue
        for p in range(NP):
            @emit("tensor")
            def _(te, c=c, p=p):
                tk.wait(te, "pe", "ve", tk.tok("ve"))
                inst = None
                for h in range(2):
                    ps_qk = pA if h == 0 else pB
                    for js in range(2):
                        inst = te.matmul(
                            ps_qk[:, js * 256:(js + 1) * 256],
                            KTIL[64 * h:64 * h + 64,
                                 p * T + c * CH + js * 128: p * T + c * CH + (js + 1) * 128],
                            KHAT[64 * h:64 * h + 64, p * CH:(p + 1) * CH],
                            start=True, stop=True, tile_position=(64 * h, 0))
                tk.inc(inst, "pe")
            qktok = tk.tok("pe")

            @emit("vector")
            def _(v, c=c, p=p, qktok=qktok):
                tk.wait(v, "ve", "pe", qktok)
                inst = None
                for h in range(2):
                    ps_qk = pA if h == 0 else pB
                    v.tensor_tensor(out=PM[:, h * 256:h * 256 + 128],
                                    in0=ps_qk[:, 0:128], in1=CONS["TRI"][:, :], op=OP.mult)
                    v.tensor_copy(out=PM[:, h * 256 + 128:h * 256 + 256],
                                  in_=ps_qk[:, 128:256])
                    inst = v.tensor_tensor(out=PM[:, 512 + h * 128:512 + h * 128 + 128],
                                           in0=ps_qk[:, 384:512], in1=CONS["TRI"][:, :], op=OP.mult)
                tk.inc(inst, "ve")
            pmtok = tk.tok("ve")

            if p3sub < 4:
                continue
            @emit("tensor")
            def _(te, c=c, p=p, pmtok=pmtok):
                tk.wait(te, "pe", "ve", pmtok)
                tk.wait(te, "pe", "se", tk.tok("se"))
                for h in range(2):
                    te.matmul(pD[64 * h:64 * h + 64, 0:256],
                              VT[:, (2 * c) * D + (2 * p + h) * 64:
                                 (2 * c) * D + (2 * p + h) * 64 + 64],
                              PM[:, h * 256:h * 256 + 256],
                              start=True, stop=False, tile_position=(0, 64 * h))
                    te.matmul(pD[64 * h:64 * h + 64, 128:256],
                              VT[:, (2 * c + 1) * D + (2 * p + h) * 64:
                                 (2 * c + 1) * D + (2 * p + h) * 64 + 64],
                              PM[:, 512 + h * 128:512 + h * 128 + 128],
                              start=False, stop=False, tile_position=(0, 64 * h))
                    te.matmul(pD[64 * h:64 * h + 64, 0:256],
                              SGT[64 * h:64 * h + 64, p * PR:p * PR + 64],
                              KHAT[64 * h:64 * h + 64, p * CH:(p + 1) * CH],
                              start=False, stop=True, tile_position=(64 * h, 64 * h))
                inst = None
                for h in range(2):
                    for isub in range(2):
                        col = 2 * (4 * p + 2 * isub + h)
                        te.matmul(pX[:, col:col + 2],
                                  PM[:, h * 256 + isub * 128:h * 256 + (isub + 1) * 128],
                                  CONS["ONES2"][:, :],
                                  start=True, stop=(isub == 0), tile_position=(0, 0))
                        if isub == 1:
                            te.matmul(pX[:, col:col + 2],
                                      PM[:, 512 + h * 128:512 + h * 128 + 128],
                                      CONS["ONES2"][:, :],
                                      start=False, stop=True, tile_position=(0, 0))
                        inst = te.matmul(pX[:, 64 + col:64 + col + 2],
                                         KHAT[64 * h:64 * h + 64,
                                              p * CH + isub * 128: p * CH + (isub + 1) * 128],
                                         SGT[64 * h:64 * h + 64, p * PR + 64:p * PR + 66],
                                         start=True, stop=True, tile_position=(64 * h, 0))
                tk.inc(inst, "pe")
            dtok = tk.tok("pe")

            if p3sub < 5:
                continue
            @emit("scalar")
            def _(s, p=p, dtok=dtok):
                tk.wait(s, "se", "pe", dtok)
                inst = s.copy(out=ATT[:, p * CH:(p + 1) * CH], in_=pD[:, 0:256])
                tk.inc(inst, "se")

        _dener = tk.tok("pe")

        @emit("vector")
        def _(v, c=c, _dener=_dener):
            tk.wait(v, "ve", "pe", _dener)
            v.tensor_copy(out=ROT1[:, 0:128], in_=pX[:, 0:128])
            vgap(v)
            v.tensor_tensor(out=ROT1[:, 0:64], in0=ROT1[:, 0:64],
                            in1=ROT1[:, 64:128], op=OP.add)
            vgap(v)
            v.tensor_scalar_add(RDE[:, 0:32],
                                ROT1[:, 0:64].rearrange("p (a b) -> p a b", b=2)[:, :, 0:1],
                                1e-6)
            vgap(v)
            inst = v.reciprocal(RDE[:, 0:32], RDE[:, 0:32])
            tk.inc(inst, "ve")
        _rec = tk.tok("ve")

        @emit("scalar")
        def _(s, c=c, _rec=_rec):
            tk.wait(s, "se", "ve", _rec)
            s.copy(out=RDB[:, 0:32], in_=RDE[:, 0:32])
            s.copy(out=RDB[:, 32:48], in_=RDE[:, 0:16])
            inst = s.copy(out=RDB[:, 48:64], in_=RDE[:, 16:32])
            tk.inc(inst, "se")
        _rdb = tk.tok("se")

        for p in range(NP):
            for isub in range(2):
                @emit("tensor")
                def _(te, p=p, isub=isub, _rdb=_rdb):
                    tk.wait(te, "pe", "se", _rdb)
                    tk.wait(te, "pe", "ve", tk.tok("ve"))
                    inst = te.matmul(pX[0:2, 128:256],
                                     RDB[:, 4 * p + 2 * isub:4 * p + 2 * isub + 2],
                                     CONS["IDENT"][:, :], start=True, stop=True)
                    tk.inc(inst, "pe")
                ttok = tk.tok("pe")

                @emit("vector")
                def _(v, ttok=ttok):
                    tk.wait(v, "ve", "pe", ttok)
                    v.tensor_copy(out=RDT[0:2, :], in_=pX[0:2, 128:256])
                    vgap(v)
                    inst = v.tensor_copy(out=RDB[:, 64:96], in_=RDE[:, 0:32])
                    tk.inc(inst, "ve")
                rdt_tok = tk.tok("ve")

                @emit("tensor")
                def _(te, rdt_tok=rdt_tok):
                    tk.wait(te, "pe", "ve", rdt_tok)
                    inst = te.matmul(pX[:, 256:384], CONS["E2"][0:2, :],
                                     RDT[0:2, 0:128], start=True, stop=True)
                    tk.inc(inst, "pe")
                btok = tk.tok("pe")

                @emit("vector")
                def _(v, p=p, isub=isub, btok=btok):
                    tk.wait(v, "ve", "pe", btok)
                    tk.wait(v, "ve", "se", tk.tok("se"))
                    sl = slice(p * CH + isub * 128, p * CH + (isub + 1) * 128)
                    inst = v.tensor_tensor(out=ATT[:, sl], in0=ATT[:, sl],
                                           in1=pX[:, 256:384], op=OP.mult)
                    tk.inc(inst, "ve")

        if p3sub < 6:
            continue
        for isub in range(2):
            @emit("tensor")
            def _(te, c=c, isub=isub):
                tk.wait(te, "pe", "ve", tk.tok("ve"))
                inst = None
                for p in range(NP):
                    lhs = ATT[:, p * CH + isub * 128: p * CH + (isub + 1) * 128]
                    te.matmul(pC[:, 0:512], lhs, wtile("Wo0", p)[:, 0:512],
                              start=(p == 0), stop=(p == NP - 1))
                    inst = te.matmul(pC[:, 512:1024], lhs, wtile("Wo0", p)[:, 512:1024],
                                     start=(p == 0), stop=(p == NP - 1))
                tk.inc(inst, "pe")
            wtok = tk.tok("pe")

            @emit("vector")
            def _(v, c=c, isub=isub, wtok=wtok):
                tk.wait(v, "ve", "pe", wtok)
                tt = 2 * c + isub
                inst = v.tensor_tensor(out=XBF[:, tt * D:(tt + 1) * D],
                                       in0=XBF[:, tt * D:(tt + 1) * D],
                                       in1=pC[:, :], op=OP.add)
                tk.inc(inst, "ve")


def _emit_phase4(nc, block, tk, emit, g):
    XBF, XQ, HQ, SLABS, WP = g["XBF"], g["POOL2"], g["POOL3"], g["POOL4"], g["WPOOL"]
    CONS, quant_pair = g["CONS"], g["quant_pair"]
    pA, pB, pC, pD, pS, pX = g["pA"], g["pB"], g["pC"], g["pD"], g["pS"], g["pX"]
    ROT1, QSB1, QSB3 = g["ROT1"], g["QSB1"], g["QSB3"]
    x_in = g["x_in"]
    ag_wg_out, ag_wu_out, ag_wd_out = g["ag_wg_out"], g["ag_wu_out"], g["ag_wd_out"]
    AG_DONE = g["AG_DONE"]
    rstd_col, nmr_col = g["rstd_col"], g["nmr_col"]

    for t in range(NT):
        @emit("sync")
        def _(sync, t=t):
            tk.wait(sync, "sync", "se", tk.tok("se"))
            tk.wait(sync, "sync", "ve", tk.tok("ve"))
            inst = sync.dma_start(out=ROT1[:, :], in_=x_in[128 * t:128 * (t + 1), :])
            tk.inc(inst, "dsp")
        dtok = tk.tok("dsp")

        @emit("scalar")
        def _(s, t=t, dtok=dtok):
            tk.wait(s, "se", "dsp", dtok)
            inst = s.activation(out=ROT1[:, :], in_=ROT1[:, :], func=AF.Identity,
                                bias=nmr_col(0, t), scale=rstd_col(0, t))
            tk.inc(inst, "se")
        stok = tk.tok("se")

        @emit("vector")
        def _(v, stok=stok):
            tk.wait(v, "ve", "se", stok)

        st = quant_pair(ROT1, D)
        for hh in range(2):
            @emit("tensor")
            def _(te, t=t, hh=hh, st=st):
                tk.wait(te, "pe", "ve", st["out"])
                tk.wait(te, "pe", "ve", tk.tok("ve"))
                inst = None
                for q in range(4):
                    dc = hh * 4 + q
                    inst = te.matmul(pX[:, q * 128:(q + 1) * 128],
                                     QSB3[:, dc * 128:(dc + 1) * 128],
                                     CONS["IDENT"][:, :], start=True, stop=True)
                tk.inc(inst, "pe")
            ptok = tk.tok("pe")

            @emit("vector")
            def _(v, t=t, hh=hh, ptok=ptok):
                tk.wait(v, "ve", "pe", ptok)
                inst = None
                for q in range(4):
                    dc = hh * 4 + q
                    inst = v.tensor_copy(
                        out=XQ[:, dc * T + t * 128: dc * T + (t + 1) * 128],
                        in_=pX[:, q * 128:(q + 1) * 128])
                tk.inc(inst, "ve")

    def wdt_tile(icI):
        if icI < 16:
            return WP[:, icI * D:(icI + 1) * D]
        return SLABS[:, 8192 + (icI - 16) * D: 8192 + (icI - 15) * D]

    SLAB_STARTS = [(0, 512), (512, 512), (1024, 512), (1536, 512),
                   (2048, 512), (2560, 256)]

    for quarter in range(4):
        for icI in range(NIC):
            @emit("sync")
            def _(sync, icI=icI):
                tk.wait(sync, "sync", "cc", AG_DONE)
                tk.wait(sync, "sync", "pe", tk.tok("pe"))
                tk.wait(sync, "sync", "ve", tk.tok("ve"))
                inst = sync.dma_start(out=wdt_tile(icI),
                                      in_=ag_wd_out[128 * icI:128 * (icI + 1), :])
                tk.inc(inst, "dsp")
        WDT_LOADED = tk.tok("dsp")

        for s_i, (s0, w) in enumerate(SLAB_STARTS):
            @emit("sync")
            def _(sync, s0=s0, w=w):
                tk.wait(sync, "sync", "cc", AG_DONE)
                tk.wait(sync, "sync", "pe", tk.tok("pe"))
                for dc in range(NP):
                    tk.inc(sync.dma_start(out=SLABS[:, dc * 512: dc * 512 + w],
                                          in_=ag_wg_out[128 * dc:128 * (dc + 1), s0:s0 + w]), "dsp")
                for dc in range(NP):
                    tk.inc(sync.dma_start(out=SLABS[:, 4096 + dc * 512: 4096 + dc * 512 + w],
                                          in_=ag_wu_out[128 * dc:128 * (dc + 1), s0:s0 + w]), "dsp")
            sl_tok = tk.tok("dsp")

            for ic in range(4):
                ti = quarter * 4 + ic

                @emit("tensor")
                def _(te, s0=s0, w=w, ti=ti, sl_tok=sl_tok):
                    tk.wait(te, "pe", "dsp", sl_tok)
                    tk.wait(te, "pe", "ve", tk.tok("ve"))
                    tk.wait(te, "pe", "se", tk.tok("se"))
                    inst = None
                    for dc in range(NP):
                        lhs = XQ[:, dc * T + ti * 128: dc * T + (ti + 1) * 128]
                        te.matmul(pA[:, 0:w], lhs, SLABS[:, dc * 512: dc * 512 + w],
                                  start=(dc == 0), stop=(dc == NP - 1))
                        inst = te.matmul(pB[:, 0:w], lhs,
                                         SLABS[:, 4096 + dc * 512: 4096 + dc * 512 + w],
                                         start=(dc == 0), stop=(dc == NP - 1))
                    tk.inc(inst, "pe")
                mtok = tk.tok("pe")

                @emit("scalar")
                def _(s, w=w, mtok=mtok):
                    tk.wait(s, "se", "pe", mtok)
                    inst = s.activation(out=QSB1[:, 0:w], in_=pA[:, 0:w], func=AF.Silu)
                    tk.inc(inst, "se")
                sltok = tk.tok("se")

                @emit("vector")
                def _(v, w=w, sltok=sltok, mtok=mtok):
                    tk.wait(v, "ve", "pe", mtok)
                    tk.wait(v, "ve", "se", tk.tok("se"))
                    tk.wait(v, "ve", "dsp", tk.tok("dsp"))
                    inst = v.tensor_tensor(out=ROT1[:, 0:w], in0=QSB1[:, 0:w],
                                           in1=pB[:, 0:w], op=OP.mult)
                    tk.inc(inst, "ve")

                st = quant_pair(ROT1, w)
                for sub in range(w // 128):
                    icI = s0 // 128 + sub

                    @emit("tensor")
                    def _(te, sub=sub, st=st):
                        tk.wait(te, "pe", "ve", st["out"])
                        tk.wait(te, "pe", "ve", tk.tok("ve"))
                        inst = te.matmul(pX[:, 384:512],
                                         QSB3[:, sub * 128:(sub + 1) * 128],
                                         CONS["IDENT"][:, :], start=True, stop=True)
                        tk.inc(inst, "pe")
                    ptok = tk.tok("pe")

                    @emit("vector")
                    def _(v, icI=icI, ic=ic, ptok=ptok):
                        tk.wait(v, "ve", "pe", ptok)
                        inst = v.tensor_copy(
                            out=HQ[:, icI * 512 + ic * 128: icI * 512 + (ic + 1) * 128],
                            in_=pX[:, 384:512])
                        tk.inc(inst, "ve")

        HQ_DONE = tk.tok("ve")
        for ic in range(4):
            ti = quarter * 4 + ic

            @emit("tensor")
            def _(te, ic=ic, ti=ti, HQ_DONE=HQ_DONE, WDT_LOADED=WDT_LOADED):
                tk.wait(te, "pe", "ve", HQ_DONE)
                tk.wait(te, "pe", "dsp", WDT_LOADED)
                tk.wait(te, "pe", "ve", tk.tok("ve"))
                inst = None
                for icI in range(NIC):
                    lhs = HQ[:, icI * 512 + ic * 128: icI * 512 + (ic + 1) * 128]
                    te.matmul(pC[:, 0:512], lhs, wdt_tile(icI)[:, 0:512],
                              start=(icI == 0), stop=(icI == NIC - 1))
                    inst = te.matmul(pC[:, 512:1024], lhs, wdt_tile(icI)[:, 512:1024],
                                     start=(icI == 0), stop=(icI == NIC - 1))
                tk.inc(inst, "pe")
            dtok2 = tk.tok("pe")

            @emit("vector")
            def _(v, ti=ti, dtok2=dtok2):
                tk.wait(v, "ve", "pe", dtok2)
                inst = v.tensor_tensor(out=XBF[:, ti * D:(ti + 1) * D],
                                       in0=XBF[:, ti * D:(ti + 1) * D],
                                       in1=pC[:, :], op=OP.add)
                tk.inc(inst, "ve")


def _emit_phase5(nc, block, tk, emit, g):
    XBF, XN2 = g["XBF"], g["POOL2"]
    FKP, FQP, WP = g["POOL3"], g["POOL4"], g["WPOOL"]
    CONS, wtile, stage_weight = g["CONS"], g["wtile"], g["stage_weight"]
    pA, pB, pC, pD, pS, pX = g["pA"], g["pB"], g["pC"], g["pD"], g["pS"], g["pX"]
    ROT1, QSB1 = g["ROT1"], g["QSB1"]
    BQC, BKC, BVC = g["BQC"], g["BKC"], g["BVC"]
    LRA, LRX, LRB, B2T, COEF = g["LRA"], g["LRX"], g["LRB"], g["B2T"], g["COEF"]
    GMX, GMR, KSC = g["GMX"], g["GMR"], g["KSC"]
    vgap = g["vgap"]
    ln_pass, build_fm = g["ln_pass"], g["build_fm"]
    y_out, gmax_b = g["y_out"], g["gmax_b"]

    p5sub = g.get("p5sub", 9)
    ln_pass(0, False)
    build_fm(XN2, 0, None, XBF)
    XN2_VE = tk.tok("ve")
    if p5sub < 2:
        return

    def fq_tile(of):
        return FQP[:, of * T:(of + 1) * T]

    def fk_tile(of):
        return FKP[:, of * T:(of + 1) * T]

    for wname, src, bias_col, out_tile in (
            ("W1q", g["wq1"], BQC, fq_tile),
            ("W1k", g["wk1"], BKC, fk_tile)):
        stage_weight(wname, src, 0, after_pe=tk.tok("pe"))
        WS = tk.tok("ge")
        for of in range(NP):
            for tt in range(4):
                @emit("tensor")
                def _(te, wname=wname, of=of, tt=tt, WS=WS):
                    tk.wait(te, "pe", "ge", WS)
                    tk.wait(te, "pe", "ve", XN2_VE)
                    tk.wait(te, "pe", "se", tk.tok("se"))
                    inst = None
                    for dc in range(NP):
                        inst = te.matmul(pA[:, 0:512],
                                         wtile(wname, dc)[:, of * 128:(of + 1) * 128],
                                         XN2[:, dc * T + tt * 512: dc * T + (tt + 1) * 512],
                                         start=(dc == 0), stop=(dc == NP - 1))
                    tk.inc(inst, "pe")
                ptok = tk.tok("pe")

                @emit("scalar")
                def _(s, of=of, tt=tt, bias_col=bias_col, out_tile=out_tile, ptok=ptok):
                    tk.wait(s, "se", "pe", ptok)
                    tk.wait(s, "se", "ve", tk.tok("ve"))
                    inst = s.activation(out=ROT1[:, 0:512], in_=pA[:, 0:512],
                                        func=AF.Square, bias=bias_col[:, of:of + 1])
                    tk.inc(inst, "se")
                    tk.wait(s, "se", "se", tk.tok("se"))
                    inst = s.activation(out=out_tile(of)[:, tt * 512:(tt + 1) * 512],
                                        in_=ROT1[:, 0:512], func=AF.Exp, scale=-0.5)
                    tk.inc(inst, "se")
    QK_SE = tk.tok("se")
    if p5sub < 3:
        return

    @emit("vector")
    def _(v):
        tk.wait(v, "ve", "se", QK_SE)
        inst = None
        for of in range(NP):
            inst = v.tensor_tensor(out=fq_tile(of), in0=fq_tile(of), in1=fk_tile(of),
                                   op=OP.mult)
        tk.inc(inst, "ve")
    PROD_VE = tk.tok("ve")
    if p5sub == 21:
        return

    for tt in range(4):
        @emit("tensor")
        def _(te, tt=tt, PROD_VE=PROD_VE):
            tk.wait(te, "pe", "ve", tk.tok("ve"))
            inst = None
            for dc in range(NP):
                inst = te.matmul(pD[0:16, 0:512], CONS["EHC"][:, dc * 16:(dc + 1) * 16],
                                 fq_tile(dc)[:, tt * 512:(tt + 1) * 512],
                                 start=(dc == 0), stop=(dc == NP - 1))
            tk.inc(inst, "pe")
        ptok = tk.tok("pe")

        @emit("vector")
        def _(v, tt=tt, ptok=ptok):
            tk.wait(v, "ve", "pe", ptok)
            inst = v.tensor_copy(out=COEF[:, tt * 512:(tt + 1) * 512], in_=pD[0:16, 0:512])
            tk.inc(inst, "ve")
    COEF_VE = tk.tok("ve")
    if p5sub < 4:
        return

    stage_weight("W1v", g["wv1"], 0, after_pe=tk.tok("pe"))
    WSV = tk.tok("ge")
    for of in range(NP):
        for tt in range(4):
            @emit("tensor")
            def _(te, of=of, tt=tt, WSV=WSV, COEF_VE=COEF_VE):
                tk.wait(te, "pe", "ge", WSV)
                tk.wait(te, "pe", "ve", COEF_VE)
                tk.wait(te, "pe", "se", tk.tok("se"))
                tk.wait(te, "pe", "ve", tk.tok("ve"))
                inst = None
                for dc in range(NP):
                    inst = te.matmul(pA[:, 0:512],
                                     wtile("W1v", dc)[:, of * 128:(of + 1) * 128],
                                     XN2[:, dc * T + tt * 512: dc * T + (tt + 1) * 512],
                                     start=(dc == 0), stop=(dc == NP - 1))
                inst = te.matmul(pB[:, 0:512], CONS["EXPC"][:, of * 128:(of + 1) * 128],
                                 COEF[:, tt * 512:(tt + 1) * 512], start=True, stop=True)
                tk.inc(inst, "pe")
            ptok = tk.tok("pe")

            @emit("scalar")
            def _(s, of=of, ptok=ptok):
                tk.wait(s, "se", "pe", ptok)
                inst = s.activation(out=QSB1[:, 0:512], in_=pA[:, 0:512],
                                    func=AF.Identity, bias=BVC[:, of:of + 1])
                tk.inc(inst, "se")
            vstok = tk.tok("se")

            @emit("vector")
            def _(v, of=of, tt=tt, vstok=vstok, ptok=ptok):
                tk.wait(v, "ve", "se", vstok)
                tk.wait(v, "ve", "pe", ptok)
                inst = v.tensor_tensor(out=FQP[:, of * T + tt * 512: of * T + (tt + 1) * 512],
                                       in0=QSB1[:, 0:512], in1=pB[:, 0:512], op=OP.mult)
                tk.inc(inst, "ve")
    QKV_VE = tk.tok("ve")
    if p5sub < 5:
        return

    for r in range(NP):
        @emit("sync")
        def _(sync, r=r):
            tk.wait(sync, "sync", "ge", tk.tok("ge"))
            tk.wait(sync, "sync", "se", tk.tok("se"))
            tk.wait(sync, "sync", "ve", tk.tok("ve"))
            inst = sync.dma_start(out=ROT1[:, 0:H], in_=g["la1"][128 * r:128 * (r + 1), :])
            tk.inc(inst, "dsp")
        dtok = tk.tok("dsp")

        @emit("gpsimd")
        def _(gp, r=r, dtok=dtok):
            tk.wait(gp, "ge", "dsp", dtok)
            inst = gp.tensor_copy(out=LRA[:, r * H:(r + 1) * H], in_=ROT1[:, 0:H])
            tk.inc(inst, "ge")

    @emit("sync")
    def _(sync):
        tk.wait(sync, "sync", "ge", tk.tok("ge"))
        inst = sync.dma_start(out=ROT1[0:16, :], in_=g["lb1"])
        tk.inc(inst, "dsp")
    _d = tk.tok("dsp")

    @emit("gpsimd")
    def _(gp, _d=_d):
        tk.wait(gp, "ge", "dsp", _d)
        tk.wait(gp, "ge", "ve", tk.tok("ve"))
        inst = gp.tensor_copy(out=LRB[:, :], in_=ROT1[0:16, :])
        tk.inc(inst, "ge")
    _g1 = tk.tok("ge")

    @emit("sync")
    def _(sync, _g1=_g1):
        tk.wait(sync, "sync", "ge", _g1)
        inst = sync.dma_start(out=ROT1[0:2, :], in_=g["bias2"])
        tk.inc(inst, "dsp")
    _d = tk.tok("dsp")

    @emit("gpsimd")
    def _(gp, _d=_d):
        tk.wait(gp, "ge", "dsp", _d)
        inst = gp.tensor_copy(out=B2T[:, :], in_=ROT1[0:2, :])
        tk.inc(inst, "ge")
    SMALLS_GE = tk.tok("ge")

    for tt in range(4):
        @emit("tensor")
        def _(te, tt=tt, SMALLS_GE=SMALLS_GE):
            tk.wait(te, "pe", "ge", SMALLS_GE)
            tk.wait(te, "pe", "ve", tk.tok("ve"))
            inst = None
            for dc in range(NP):
                inst = te.matmul(pB[0:16, 0:512], LRA[:, dc * H:(dc + 1) * H],
                                 XN2[:, dc * T + tt * 512: dc * T + (tt + 1) * 512],
                                 start=(dc == 0), stop=(dc == NP - 1))
            tk.inc(inst, "pe")
        ptok = tk.tok("pe")

        @emit("vector")
        def _(v, tt=tt, ptok=ptok):
            tk.wait(v, "ve", "pe", ptok)
            inst = v.tensor_copy(out=LRX[:, tt * 512:(tt + 1) * 512], in_=pB[0:16, 0:512])
            tk.inc(inst, "ve")
    LRX_VE = tk.tok("ve")
    if p5sub < 6:
        return

    @emit("vector")
    def _(v):
        tk.wait(v, "ve", "pe", tk.tok("pe"))
        inst = v.memset(GMX[:], 0.0)
        tk.inc(inst, "ve")
    for r in range(NP):
        @emit("sync")
        def _(sync, r=r):
            tk.wait(sync, "sync", "ve", tk.tok("ve"))
            tk.wait(sync, "sync", "ge", tk.tok("ge"))
            inst = sync.dma_start(out=ROT1[:, :], in_=g["wk1f"][128 * r:128 * (r + 1), :])
            tk.inc(inst, "dsp")
        dtok = tk.tok("dsp")

        @emit("vector")
        def _(v, r=r, dtok=dtok):
            tk.wait(v, "ve", "dsp", dtok)
            v.tensor_reduce(out=GMX[:, 1:2], in_=ROT1[:, :], axis=mybir.AxisListType.X,
                            op=OP.max, apply_absolute_value=True)
            vgap(v)
            inst = v.tensor_max(GMX[:, 0:1], GMX[:, 0:1], GMX[:, 1:2])
            tk.inc(inst, "ve")
            vgap(v)
    _v = tk.tok("ve")

    @emit("gpsimd")
    def _(gp, _v=_v):
        tk.wait(gp, "ge", "ve", _v)
        inst = gp.dma_start(out=gmax_b[0, :], in_=GMX[:, 0:1])
        tk.inc(inst, "dpool")
        tk.wait(gp, "ge", "dpool", tk.tok("dpool"))
        inst = gp.dma_start(out=GMR, in_=gmax_b)
        tk.inc(inst, "dpool")
    _d2 = tk.tok("dpool")

    @emit("vector")
    def _(v, _d2=_d2):
        tk.wait(v, "ve", "dpool", _d2)
        v.reduce_max(GMR[0:1, 0:1], GMR[0:1, :], axis=mybir.AxisListType.X)
        vgap(v)
        v.tensor_scalar_mul(GMR[0:1, 1:2], GMR[0:1, 0:1], 1.0 / 7.0)
        vgap(v)
        inst = v.reciprocal(GMR[0:1, 0:1], GMR[0:1, 1:2])
        tk.inc(inst, "ve")
    _v2 = tk.tok("ve")

    @emit("tensor")
    def _(te, _v2=_v2):
        tk.wait(te, "pe", "ve", _v2)
        inst = te.matmul(pX[:, 384:386], CONS["ONEF"][0:1, :], GMR[0:1, 0:2],
                         start=True, stop=True)
        tk.inc(inst, "pe")
    _pks = tk.tok("pe")

    @emit("scalar")
    def _(s, _pks=_pks):
        tk.wait(s, "se", "pe", _pks)
        s.copy(out=KSC[:, 0:2], in_=pX[:, 384:386])
        s.copy(out=QSB1[:, 32:64], in_=QSB1[:, 0:32])
        inst = s.copy(out=QSB1[:, 64:96], in_=QSB1[:, 0:32])
        tk.inc(inst, "se")
    KSC_VE = tk.tok("ve")

    stage_weight("W1o", g["wo1"], 0, after_pe=tk.tok("pe"))
    for r in range(NP):
        @emit("sync")
        def _(sync, r=r):
            tk.wait(sync, "sync", "ve", tk.tok("ve"))
            tk.wait(sync, "sync", "ge", tk.tok("ge"))
            inst = sync.dma_start(out=ROT1[:, :], in_=g["wk1f"][128 * r:128 * (r + 1), :])
            tk.inc(inst, "dsp")
        dtok = tk.tok("dsp")

        @emit("vector")
        def _(v, r=r, dtok=dtok, KSC_VE=KSC_VE):
            def fence(inst):
                tk.inc(inst, "ve")
                tk.wait(v, "ve", "ve", tk.tok("ve"))
            tk.wait(v, "ve", "dsp", dtok)
            tk.wait(v, "ve", "se", tk.tok("se"))
            fence(v.tensor_scalar_mul(ROT1[:, :], ROT1[:, :], KSC[:, 0:1]))
            fence(v.tensor_copy(out=ROT1[:, :].bitcast(I32), in_=ROT1[:, :]))
            fence(v.tensor_copy(out=ROT1[:, :], in_=ROT1[:, :].bitcast(I32)))
            fence(v.tensor_scalar(out=ROT1[:, :], in0=ROT1[:, :], scalar1=-8.0,
                                  scalar2=7.0, op0=OP.max, op1=OP.min))
            inst = v.tensor_scalar_mul(WP[:, (8 + r) * D:(9 + r) * D], ROT1[:, :],
                                       KSC[:, 1:2])
            tk.inc(inst, "ve")
    K1_VE = tk.tok("ve")
    W1O_GE = tk.tok("ge")
    if p5sub < 7:
        return

    for ti in range(NT):
        @emit("tensor")
        def _(te, ti=ti):
            tk.wait(te, "pe", "ve", tk.tok("ve"))
            tk.wait(te, "pe", "ge", W1O_GE)
            inst = None
            for dc in range(NP):
                lhs = FQP[:, dc * T + ti * 128: dc * T + (ti + 1) * 128]
                te.matmul(pC[:, 0:512], lhs, wtile("W1o", dc)[:, 0:512],
                          start=(dc == 0), stop=False)
                te.matmul(pC[:, 512:1024], lhs, wtile("W1o", dc)[:, 512:1024],
                          start=(dc == 0), stop=False)
            for dc in range(NP):
                lhs = XN2[:, dc * T + ti * 128: dc * T + (ti + 1) * 128]
                te.matmul(pC[:, 0:512], lhs, WP[:, (8 + dc) * D:(8 + dc) * D + 512],
                          start=False, stop=False)
                te.matmul(pC[:, 512:1024], lhs, WP[:, (8 + dc) * D + 512:(9 + dc) * D],
                          start=False, stop=False)
            lhs = LRX[:, ti * 128:(ti + 1) * 128]
            te.matmul(pC[:, 0:512], lhs, LRB[:, 0:512], start=False, stop=False)
            te.matmul(pC[:, 512:1024], lhs, LRB[:, 512:1024], start=False, stop=False)
            te.matmul(pC[:, 0:512], CONS["ONESR2"][:, :], B2T[:, 0:512],
                      start=False, stop=True)
            inst = te.matmul(pC[:, 512:1024], CONS["ONESR2"][:, :], B2T[:, 512:1024],
                             start=False, stop=True)
            tk.inc(inst, "pe")
        otok = tk.tok("pe")

        rot = ROT1

        @emit("vector")
        def _(v, ti=ti, otok=otok, rot=rot):
            tk.wait(v, "ve", "pe", otok)
            tk.wait(v, "ve", "dpool", tk.tok("dpool"))
            inst = v.tensor_tensor(out=rot[:, :], in0=XBF[:, ti * D:(ti + 1) * D],
                                   in1=pC[:, :], op=OP.add)
            tk.inc(inst, "ve")
        etok = tk.tok("ve")

        @emit("gpsimd")
        def _(gp, ti=ti, etok=etok, rot=rot):
            tk.wait(gp, "ge", "ve", etok)
            inst = gp.dma_start(out=y_out[128 * ti:128 * (ti + 1), :], in_=rot[:, :])
            tk.inc(inst, "dpool")


def make_in_maps(inputs):
    consts = _host_consts()
    full_x = np.ascontiguousarray(np.asarray(inputs["L0_attn_x"], np.float32).reshape(NCORES * T, D))
    in_maps = []
    for c in range(NCORES):
        m = {}
        m["x_slice"] = full_x[c * T:(c + 1) * T]
        for k in ("L0_attn_Wq", "L0_attn_Wk", "L0_attn_Wv", "L0_attn_Wo", "L0_attn_Wa",
                  "L1_attn_Wq", "L1_attn_Wk", "L1_attn_Wv", "L1_attn_Wo",
                  "L1_ffn_kernel", "L1_ffn_a", "L1_ffn_b"):
            m[k] = np.ascontiguousarray(np.asarray(inputs[k], np.float32))
        m["ba_row2"] = np.ascontiguousarray(
            np.stack([np.asarray(inputs["L0_attn_ba"], np.float32),
                      np.full(H, 2.0, np.float32)]))
        m["wg_shard"] = np.ascontiguousarray(inputs["L0_ffn_w_gate"][:, 128 * c:128 * (c + 1)])
        m["wu_shard"] = np.ascontiguousarray(inputs["L0_ffn_w_up"][:, 128 * c:128 * (c + 1)])
        m["wd_shard"] = np.ascontiguousarray(inputs["L0_ffn_w_down"][:, 352 * c:352 * (c + 1)])
        m["bq_col"] = np.ascontiguousarray(np.asarray(inputs["L1_attn_bq"], np.float32).reshape(NP, 128).T)
        m["bk_col"] = np.ascontiguousarray(np.asarray(inputs["L1_attn_bk"], np.float32).reshape(NP, 128).T)
        m["bv_col"] = np.ascontiguousarray(np.asarray(inputs["L1_attn_bv"], np.float32).reshape(NP, 128).T)
        m["bias2"] = np.ascontiguousarray(np.stack([
            np.asarray(inputs["L1_ffn_bias"], np.float32),
            np.asarray(inputs["L1_attn_bo"], np.float32)]))
        for k, v in consts.items():
            m["c_" + k] = np.ascontiguousarray(v)
        oh = np.zeros((128, 8), np.float32)
        oh[:, c ^ 1] = 1.0
        m["partner_onehot"] = oh
        m["state_mask"] = np.full((128, 1), 1.0 if c % 2 == 1 else 0.0, np.float32)
        in_maps.append(m)
    return in_maps


def kernel(**inputs):
    nc, _ = build_program()
    in_maps = make_in_maps(inputs)
    res = run_bass_kernel_spmd(nc, in_maps, core_ids=list(range(NCORES)))
    out = np.concatenate([r["y_slice"] for r in res.results], axis=0)
    return np.ascontiguousarray(out.reshape(4, 4096, 1024))


